# version 04
# baseline (speedup 1.0000x reference)
import numpy as np
import ml_dtypes
import contextlib
import concourse.bass as bass
import concourse.mybir as mybir
from concourse.bass_utils import run_bass_kernel_spmd

F32 = mybir.dt.float32
BF16 = mybir.dt.bfloat16
I32 = mybir.dt.int32
ALU = mybir.AluOpType
AF = mybir.ActivationFunctionType
AX = mybir.AxisListType.X

NCORES = 8
B, S, D, H = 2, 4096, 2048, 8192
TPC = B * S // NCORES
NT = TPC // 128
KD = D // 128
KH = H // 128
NH = H // 512
ND = D // 512
NHEAD = 4
NQ = S // 512
SLICE = H // NCORES
ISQ = 1.0 / float(np.sqrt(np.float64(128.0)))
TGRP = 4
NGRP = NT // TGRP
HGRP = 2
GROUPS_ALL = [list(range(NCORES))]
GROUPS_BATCH = [[0, 1, 2, 3], [4, 5, 6, 7]]


class Sched:
    def __init__(self):
        self.steps = {e: [] for e in ("sync", "scalar", "vector", "tensor", "gpsimd")}
        self.val = {}

    def emit(self, engine, fn, waits=(), inc=None, inc_n=1, tag=""):
        self.steps[engine].append((fn, [w for w in waits if w is not None], inc, inc_n, tag))
        if inc is not None:
            self.val[inc] = self.val.get(inc, 0) + inc_n
            return (inc, self.val[inc])
        return None

    def dma(self, engine, fn, chan, slot, waits=(), tag=""):
        s = chan[slot % len(chan)]
        return self.emit(engine, fn, waits=waits, inc=s, inc_n=16, tag=tag)

    def now(self, sem):
        return (sem, self.val.get(sem, 0))

    def build(self, block):
        def mk(name):
            def body(eng):
                force_drain = name in ("vector", "scalar")
                for fn, waits, inc, inc_n, tag in self.steps[name]:
                    merged = {}
                    for (s, v) in waits:
                        if v > 0 and v > merged.get(s, 0):
                            merged[s] = v
                    for s, v in merged.items():
                        eng.wait_ge(s, v)
                    if force_drain:
                        eng.drain()
                    instr = fn(eng)
                    if inc is not None:
                        instr.then_inc(inc, inc_n)
            return body
        block.sync(mk("sync"))
        block.scalar(mk("scalar"))
        block.vector(mk("vector"))
        block.tensor(mk("tensor"))
        block.gpsimd(mk("gpsimd"))


def lst(x):
    return x if isinstance(x, list) else ([x] if x is not None else [])


def build_nc():
    nc = bass.Bass("TRN2", target_bir_lowering=False)
    x_in = nc.dram_tensor("x", [TPC, D], F32, kind="ExternalInput")
    gw_in = nc.dram_tensor("gw", [SLICE, D], F32, kind="ExternalInput")
    vw_in = nc.dram_tensor("vw", [SLICE, D], F32, kind="ExternalInput")
    ow_in = nc.dram_tensor("ow", [D, SLICE], F32, kind="ExternalInput")
    gb_in = nc.dram_tensor("gb", [1, H], F32, kind="ExternalInput")
    vb_in = nc.dram_tensor("vb", [1, H], F32, kind="ExternalInput")
    ob_in = nc.dram_tensor("ob", [1, D], F32, kind="ExternalInput")
    wq_in = nc.dram_tensor("wq", [D, NHEAD * 128], F32, kind="ExternalInput")
    wk_in = nc.dram_tensor("wk", [D, NHEAD * 128], F32, kind="ExternalInput")
    wv_in = nc.dram_tensor("wv", [D, NHEAD * 128], F32, kind="ExternalInput")
    wo_in = nc.dram_tensor("wo", [NHEAD * 128, D], F32, kind="ExternalInput")
    cs_in = nc.dram_tensor("cs", [128, S], F32, kind="ExternalInput")
    ss_in = nc.dram_tensor("ss2", [128, S], F32, kind="ExternalInput")
    mk_in = nc.dram_tensor("mask", [4, 128, 512], BF16, kind="ExternalInput")
    out_d = nc.dram_tensor("out", [TPC, D], F32, kind="ExternalOutput")

    x1_d = nc.dram_tensor("x1d", [NT, 128, D], F32)
    x2_d = nc.dram_tensor("x2d", [NT, 128, D], F32)
    gq_in = nc.dram_tensor("gq_in", [KD * 128, SLICE], BF16)
    vq_in = nc.dram_tensor("vq_in", [KD * 128, SLICE], BF16)
    oq_in = nc.dram_tensor("oq_in", [SLICE, D], BF16)
    gq_all = nc.dram_tensor("gq_all", [NCORES * KD * 128, SLICE], BF16, addr_space="Shared")
    vq_all = nc.dram_tensor("vq_all", [NCORES * KD * 128, SLICE], BF16, addr_space="Shared")
    oq_all = nc.dram_tensor("oq_all", [H, D], BF16, addr_space="Shared")
    gsum_in = nc.dram_tensor("gsum_in", [1, 4], F32)
    gsum_out = nc.dram_tensor("gsum_out", [1, 4], F32, addr_space="Shared")
    bq_d = nc.dram_tensor("bq_d", [1, 2 * H], BF16)
    y_in = nc.dram_tensor("y_pub", [KD * 128, TPC], BF16)
    y_all = nc.dram_tensor("y_all", [4 * KD * 128, TPC], BF16)
    pp_d = nc.dram_tensor("pp_d", [S, D], BF16)
    rs_d = nc.dram_tensor("rs_d", [TPC, D], BF16)
    wqkb_d = nc.dram_tensor("wqkb_d", [KD * 128, 2 * NHEAD * 128], BF16)
    wvb_d = nc.dram_tensor("wvb_d", [KD * 128, NHEAD * 128], BF16)
    wob_d = nc.dram_tensor("wob_d", [NHEAD * 128, D], BF16)

    st_ctx = contextlib.ExitStack()
    E = st_ctx.enter_context
    with st_ctx:
        xw    = E(nc.sbuf_tensor("xw", [128, 2 * D], F32))
        resb  = E(nc.sbuf_tensor("resb", [128, 2 * D], F32))
        junk  = E(nc.sbuf_tensor("junkx", [128, 8], F32))
        st8   = E(nc.sbuf_tensor("st8x", [128, 16], F32))
        ascl  = E(nc.sbuf_tensor("asclx", [128, NT * 4], F32))
        q32   = E(nc.sbuf_tensor("q32x", [128, 1024], I32))
        qb8   = E(nc.sbuf_tensor("qb8x", [128, D], BF16))
        auxp  = E(nc.sbuf_tensor("auxpx", [128, 128], BF16))
        auxG  = E(nc.sbuf_tensor("auxGx", [128, NT * 128], BF16))
        auxV  = E(nc.sbuf_tensor("auxVx", [128, NT * 128], BF16))
        auxO  = E(nc.sbuf_tensor("auxOx", [128, NT * 128], BF16))
        bo    = E(nc.sbuf_tensor("box", [1, D], BF16))
        bgt   = E(nc.sbuf_tensor("bgtx", [1, 2 * 512], BF16))
        bvt   = E(nc.sbuf_tensor("bvtx", [1, 2 * 512], BF16))
        wstr  = E(nc.sbuf_tensor("wstrx", [128, 4 * 512], BF16))
        ostr  = E(nc.sbuf_tensor("ostrx", [128, 2 * 512], BF16))
        vpre  = E(nc.sbuf_tensor("vprex", [128, 512], F32))
        gpre  = E(nc.sbuf_tensor("gprex", [128, 512], F32))
        onesf = E(nc.sbuf_tensor("onesfx", [128, 1], F32))
        ones1f = E(nc.sbuf_tensor("ones1fx", [1, 128], F32))
        ones1b = E(nc.sbuf_tensor("ones1bx", [1, 128], BF16))
        onescb = E(nc.sbuf_tensor("onescbx", [128, 1], BF16))
        gam   = E(nc.sbuf_tensor("gamx", [1, 8], F32))
        gamP  = E(nc.sbuf_tensor("gamPx", [128, 8], F32))
        wsum  = E(nc.sbuf_tensor("wsumx", [128, 4], F32))
        wtmp  = E(nc.sbuf_tensor("wtmpx", [128, 4], F32))
        sc128 = E(nc.sbuf_tensor("sc128x", [128, 1], F32))
        wtr   = E(nc.sbuf_tensor("wtrx", [128, 4 * 128], BF16))
        dumpb = E(nc.sbuf_tensor("dumpbx", [128, D], BF16))
        PS = [E(nc.psum_tensor(f"ps{i}", [128, 512], F32)) for i in range(8)]

        ch_x  = [E(nc.semaphore(f"chx{i}")) for i in range(2)]
        ch_w  = [E(nc.semaphore(f"chw{i}")) for i in range(2)]
        ch_g  = [E(nc.semaphore(f"chg{i}")) for i in range(2)]
        ch_o  = [E(nc.semaphore(f"cho{i}")) for i in range(2)]
        ch_tr = [E(nc.semaphore(f"chtr{i}")) for i in range(4)]
        ch_gp = [E(nc.semaphore(f"chgp{i}")) for i in range(4)]
        ch_b  = [E(nc.semaphore("chb0"))]
        ch_bi = [E(nc.semaphore(f"chbi{i}")) for i in range(2)]
        ch_cc = [E(nc.semaphore("chcc0"))]
        ch_st = [E(nc.semaphore(f"chst{i}")) for i in range(8)]
        ch_sw = [E(nc.semaphore(f"chsw{i}")) for i in range(2)]
        s_v = E(nc.semaphore("s_vx"))
        s_s = E(nc.semaphore("s_sx"))
        s_t = E(nc.semaphore("s_tx"))
        block = E(nc.Block())
        SQ = Sched()

        def collective(kind, op, ins, outs, groups, waits=()):
            return SQ.emit("gpsimd", lambda e, kind=kind, op=op, ins=ins, outs=outs, groups=groups:
                           e.collective_compute(kind, op, replica_groups=groups, ins=ins, outs=outs),
                           waits=waits, inc=ch_cc[0], inc_n=1, tag=f"cc_{kind}")

        G = {"junk": None, "auxp": None, "qb8": None, "res": [], "x": {0: None, 1: None},
             "tr": {0: None, 1: None, 2: None, 3: None}}

        SQ.emit("vector", lambda e: e.memset(onesf[:, :], 1.0))
        SQ.emit("vector", lambda e: e.memset(ones1f[:, :], 1.0))
        SQ.emit("vector", lambda e: e.memset(ones1b[:, :], 1.0))
        SQ.emit("vector", lambda e: e.memset(onescb[:, :], 1.0))
        SQ.emit("vector", lambda e: e.memset(sc128[:, :], ISQ))
        SQ.emit("vector", lambda e: e.memset(auxp[:, :], 0.0))
        SQ.emit("vector", lambda e: e.memset(gam[:, :], 0.0))
        v_init = SQ.emit("vector", lambda e: e.memset(wsum[:, :], 0.0), inc=s_v)
        b_all = SQ.dma("gpsimd", lambda e: e.dma_start(out=bo[:, :], in_=ob_in[:, :]), ch_b, 0)
        SQ.dma("gpsimd", lambda e: e.dma_start(out=bq_d[0:1, 0:H], in_=gb_in[:, :]), ch_b, 0)
        bq_done = SQ.dma("gpsimd", lambda e: e.dma_start(out=bq_d[0:1, H:2*H], in_=vb_in[:, :]), ch_b, 0)
        for k in range(NHEAD):
            SQ.dma("gpsimd", lambda e, k=k: e.dma_start(out=wob_d[k*128:(k+1)*128, :],
                    in_=wo_in[k*128:(k+1)*128, :]), ch_b, 0)
        for k in range(KD):
            SQ.dma("gpsimd", lambda e, k=k: e.dma_start(out=wqkb_d[k*128:(k+1)*128, 0:NHEAD*128],
                    in_=wq_in[k*128:(k+1)*128, :]), ch_b, 0)
            SQ.dma("gpsimd", lambda e, k=k: e.dma_start(out=wqkb_d[k*128:(k+1)*128, NHEAD*128:2*NHEAD*128],
                    in_=wk_in[k*128:(k+1)*128, :]), ch_b, 0)
            SQ.dma("gpsimd", lambda e, k=k: e.dma_start(out=wvb_d[k*128:(k+1)*128, :],
                    in_=wv_in[k*128:(k+1)*128, :]), ch_b, 0)

        aw_done = SQ.now(ch_b[0])

        with contextlib.ExitStack() as wctx:
            wf = wctx.enter_context(nc.sbuf_tensor("wfx", [128, 2 * D], F32))
            wbq = wctx.enter_context(nc.sbuf_tensor("wbx", [128, 2 * D], BF16))
            wmats = [("g", gw_in, SLICE // 128, D), ("v", vw_in, SLICE // 128, D), ("o", ow_in, D // 128, SLICE)]
            chunk_v = {}
            wcc = 0
            for mi, (m, win, nch, width) in enumerate(wmats):
                for c in range(nch):
                    buf = wcc % 2
                    w = [chunk_v[wcc - 2]] if wcc >= 2 else [v_init]
                    ld = SQ.dma("sync", lambda e, win=win, c=c, buf=buf, width=width:
                                e.dma_start(out=wf[:, buf*D:buf*D+width], in_=win[c*128:(c+1)*128, :]),
                                ch_w, buf, waits=w, tag=f"sumload {m}{c}")
                    SQ.emit("vector", lambda e, buf=buf, width=width, mi=mi:
                            e.reduce_sum(wtmp[:, mi:mi+1], wf[:, buf*D:buf*D+width], axis=AX,
                                         apply_absolute_value=True), waits=[ld])
                    chunk_v[wcc] = SQ.emit("vector", lambda e, mi=mi:
                            e.tensor_add(wsum[:, mi:mi+1], wsum[:, mi:mi+1], wtmp[:, mi:mi+1]), inc=s_v)
                    wcc += 1
            t_ps = SQ.emit("tensor", lambda e: e.matmul(PS[0][0:1, 0:4], onesf[:, :], wsum[:, :],
                           start=True, stop=True), waits=[SQ.now(s_v)], inc=s_t)
            v_ps = SQ.emit("vector", lambda e: e.tensor_copy(wtmp[0:1, 0:4], PS[0][0:1, 0:4]),
                           waits=[t_ps], inc=s_v)
            g_ps = SQ.dma("gpsimd", lambda e: e.dma_start(out=gsum_in[:, :], in_=wtmp[0:1, 0:4]),
                          ch_gp, 0, waits=[v_ps])
            cc_sum = collective("AllReduce", ALU.add, [gsum_in[:, :]], [gsum_out[:, :]], GROUPS_ALL,
                                waits=[g_ps])
            g_ld = SQ.dma("gpsimd", lambda e: e.dma_start(out=wtmp[0:1, 0:4], in_=gsum_out[:, :]),
                          ch_b, 0, waits=[cc_sum])
            counts = [float(H * D), float(H * D), float(D * H)]
            for mi in range(3):
                SQ.emit("vector", lambda e, mi=mi: e.tensor_scalar(gam[0:1, mi:mi+1], wtmp[0:1, mi:mi+1],
                        1.0 / counts[mi], 1e-5, op0=ALU.mult, op1=ALU.max), waits=[g_ld] if mi == 0 else [])
            SQ.emit("vector", lambda e: e.reciprocal(gam[0:1, 4:7], gam[0:1, 0:3]))
            v_gam = SQ.emit("vector", lambda e: e.tensor_copy(gam[0:1, 7:8], gam[0:1, 0:1]), inc=s_v)
            t_bc0 = SQ.emit("tensor", lambda e: e.matmul(PS[1][:, 0:8], ones1f[:, :], gam[0:1, 0:8],
                            start=True, stop=True), waits=[v_gam], inc=s_t)
            t_bc = SQ.emit("vector", lambda e: e.tensor_copy(gamP[:, 0:8], PS[1][:, 0:8]), waits=[t_bc0], inc=s_v)

            q_v = {}
            st_done = {}
            tr_last = {}
            tr_idx = 0
            for mi, (m, win, nch, width) in enumerate(wmats):
                nblk = width // 128
                for c in range(nch):
                    buf = wcc % 2
                    w = [q_v.get(wcc - 2, chunk_v.get(wcc - 2)), t_bc]
                    ld = SQ.dma("sync", lambda e, win=win, c=c, buf=buf, width=width:
                                e.dma_start(out=wf[:, buf*D:buf*D+width], in_=win[c*128:(c+1)*128, :]),
                                ch_w, buf, waits=w, tag=f"qload {m}{c}")
                    for cq in range(width // 1024):
                        SQ.emit("vector", lambda e, buf=buf, cq=cq, mi=mi:
                                e.tensor_scalar(q32[:, :], wf[:, buf*D + cq*1024: buf*D + (cq+1)*1024],
                                                gamP[:, 4+mi:5+mi], None, op0=ALU.mult),
                                waits=[ld, t_bc] if cq == 0 else [])
                        q_v[wcc] = SQ.emit("vector", lambda e, buf=buf, cq=cq:
                                e.tensor_scalar(wbq[:, buf*D + cq*1024: buf*D + (cq+1)*1024], q32[:, :],
                                                -1.0, 1.0, op0=ALU.max, op1=ALU.min),
                                waits=tr_last.get(wcc - 2, []) if cq == 0 else [], inc=s_v)
                    slot_last = {}
                    for k in range(nblk):
                        slot = tr_idx % 4
                        w2 = [q_v[wcc]] if k == 0 else []
                        if tr_idx >= 4:
                            w2.append(st_done[tr_idx - 4])
                        lt = SQ.dma("scalar", lambda e, buf=buf, k=k, slot=slot:
                                    e.dma_start_transpose(wtr[:, slot*128:(slot+1)*128],
                                                          wbq[:, buf*D + k*128: buf*D + (k+1)*128]),
                                    ch_tr, slot, waits=w2, tag=f"T {m}{c}.{k}")
                        dstt = [gq_in, vq_in, oq_in][mi]
                        st_done[tr_idx] = SQ.dma("gpsimd", lambda e, dstt=dstt, k=k, c=c, slot=slot:
                                e.dma_start(out=dstt[k*128:(k+1)*128, c*128:(c+1)*128],
                                            in_=wtr[:, slot*128:(slot+1)*128]),
                                ch_gp, slot, waits=[lt], tag=f"W {m}{c}.{k}")
                        slot_last[slot] = st_done[tr_idx]
                        tr_idx += 1
                    tr_last[wcc] = list(slot_last.values())
                    wcc += 1
            wq_done = [SQ.now(s) for s in ch_gp]
            cc_g = collective("AllGather", ALU.bypass, [gq_in[:, :]], [gq_all[:, :]], GROUPS_ALL, waits=wq_done)
            cc_vv = collective("AllGather", ALU.bypass, [vq_in[:, :]], [vq_all[:, :]], GROUPS_ALL)
            cc_o = collective("AllGather", ALU.bypass, [oq_in[:, :]], [oq_all[:, :]], GROUPS_ALL)
            for sl in range(4):
                G["tr"][sl] = wq_done

        def gv_tile_fn(dst_ap, which, k, n):
            hid0 = n * 512
            slab = hid0 // SLICE
            off = hid0 % SLICE
            src = gq_all if which == 0 else vq_all
            r0 = slab * KD * 128 + k * 128
            return lambda e: e.dma_start(out=dst_ap, in_=src[r0:r0+128, off:off+512])

        def ln_stats(xbuf, waits_first):
            sc = SQ.emit("scalar", lambda e, xbuf=xbuf: e.activation(dumpb[:, :], xw[:, xbuf*D:(xbuf+1)*D],
                    AF.Square, accum_out=st8[:, 0:1]), waits=waits_first, inc=s_s)
            SQ.emit("vector", lambda e, xbuf=xbuf: e.reduce_sum(st8[:, 1:2], xw[:, xbuf*D:(xbuf+1)*D], axis=AX),
                    waits=waits_first)
            SQ.emit("vector", lambda e: e.tensor_scalar(st8[:, 2:3], st8[:, 1:2], 1.0/D, None, op0=ALU.mult))
            SQ.emit("vector", lambda e: e.tensor_scalar(st8[:, 3:4], st8[:, 0:1], 1.0/D, None, op0=ALU.mult),
                    waits=[sc])
            SQ.emit("vector", lambda e: e.tensor_mul(st8[:, 4:5], st8[:, 2:3], st8[:, 2:3]))
            SQ.emit("vector", lambda e: e.tensor_sub(st8[:, 3:4], st8[:, 3:4], st8[:, 4:5]))
            v_var = SQ.emit("vector", lambda e: e.tensor_scalar(st8[:, 3:4], st8[:, 3:4], 1e-5, None,
                    op0=ALU.add), inc=s_v)
            s_ln = SQ.emit("scalar", lambda e: e.activation(st8[:, 4:5], st8[:, 3:4], AF.Ln), waits=[v_var], inc=s_s)
            v_l2 = SQ.emit("vector", lambda e: e.tensor_scalar(st8[:, 4:5], st8[:, 4:5], -0.5, None,
                    op0=ALU.mult), waits=[s_ln], inc=s_v)
            return SQ.emit("scalar", lambda e: e.activation(st8[:, 4:5], st8[:, 4:5], AF.Exp), waits=[v_l2], inc=s_s)

        def ffn(app, xi_src, res_dst, extra_first=()):
            with contextlib.ExitStack() as fctx:
                A2 = fctx.enter_context
                yqT = A2(nc.sbuf_tensor(f"yqT{app}", [128, TGRP * KD * 128], BF16))
                hbuf = A2(nc.sbuf_tensor(f"hbuf{app}", [128, TGRP * H], BF16))
                hqT = A2(nc.sbuf_tensor(f"hqT{app}", [128, HGRP * KH * 128], BF16))
                last_group_gv = None
                first = list(extra_first)
                omm = {}
                bias_tok = {}
                for grp in range(NGRP):
                    tiles = list(range(grp * TGRP, (grp + 1) * TGRP))
                    yqT_done = {}
                    for i, t in enumerate(tiles):
                        slot = i
                        xbuf = t % 2
                        xl = SQ.dma("sync", lambda e, t=t, xbuf=xbuf: e.dma_start(
                                out=xw[:, xbuf*D:(xbuf+1)*D], in_=xi_src(t)), ch_x, xbuf,
                                waits=[G["x"][xbuf]] + first, tag=f"xA{app}.{t}")
                        first = []
                        s_rstd = ln_stats(xbuf, [xl, G["junk"]])
                        v_y = SQ.emit("vector", lambda e, xbuf=xbuf: e.tensor_scalar(
                                xw[:, xbuf*D:(xbuf+1)*D], xw[:, xbuf*D:(xbuf+1)*D], st8[:, 2:3], st8[:, 4:5],
                                op0=ALU.subtract, op1=ALU.mult), waits=[s_rstd], inc=s_v)
                        SQ.emit("vector", lambda e, xbuf=xbuf: e.reduce_max(st8[:, 5:6], xw[:, xbuf*D:(xbuf+1)*D],
                                axis=AX, apply_absolute_value=True))
                        SQ.emit("vector", lambda e: e.tensor_scalar(st8[:, 5:6], st8[:, 5:6], 1e-5, None, op0=ALU.max))
                        SQ.emit("vector", lambda e: e.reciprocal(st8[:, 6:7], st8[:, 5:6]))
                        SQ.emit("vector", lambda e: e.tensor_scalar(st8[:, 6:7], st8[:, 6:7], 127.0, None, op0=ALU.mult))
                        SQ.emit("vector", lambda e: e.reciprocal(st8[:, 7:8], st8[:, 6:7]))
                        v_yqb = None
                        for cq in range(2):
                            SQ.emit("vector", lambda e, xbuf=xbuf, cq=cq: e.tensor_scalar(q32[:, :],
                                    xw[:, xbuf*D + cq*1024: xbuf*D + (cq+1)*1024],
                                    st8[:, 6:7], None, op0=ALU.mult))
                            v_yqb = SQ.emit("vector", lambda e, cq=cq: e.tensor_copy(
                                    qb8[:, cq*1024:(cq+1)*1024], q32[:, :]),
                                    waits=[G["qb8"]] if cq == 0 else [], inc=s_v)
                        G["x"][xbuf] = v_yqb
                        SQ.emit("vector", lambda e, t=t: e.tensor_mul(ascl[:, t*4+0:t*4+1], st8[:, 7:8],
                                gamP[:, 0:1]), waits=[t_bc])
                        SQ.emit("vector", lambda e, t=t: e.tensor_mul(ascl[:, t*4+1:t*4+2], st8[:, 7:8],
                                gamP[:, 1:2]))
                        SQ.emit("vector", lambda e: e.tensor_mul(junk[:, 0:1], st8[:, 6:7], gamP[:, 4:5]),
                                waits=[G["junk"]])
                        v_a1 = SQ.emit("vector", lambda e: e.tensor_copy(auxp[:, 0:1], junk[:, 0:1]),
                                waits=[G["auxp"]], inc=s_v)
                        tr1 = SQ.dma("scalar", lambda e, t=t: e.dma_start_transpose(
                                auxG[:, t*128:(t+1)*128], auxp[:, :]), ch_tr, slot,
                                waits=[v_a1] + lst(G["tr"][slot]))
                        SQ.emit("vector", lambda e: e.tensor_mul(junk[:, 0:1], st8[:, 6:7], gamP[:, 5:6]),
                                waits=[tr1])
                        v_a2 = SQ.emit("vector", lambda e: e.tensor_copy(auxp[:, 0:1], junk[:, 0:1]), inc=s_v)
                        SQ.dma("scalar", lambda e, t=t: e.dma_start_transpose(
                                auxV[:, t*128:(t+1)*128], auxp[:, :]), ch_tr, slot, waits=[v_a2])
                        G["junk"] = v_a2
                        last = None
                        for k in range(KD):
                            last = SQ.dma("scalar", lambda e, i=i, k=k: e.dma_start_transpose(
                                    yqT[:, (i*KD+k)*128:(i*KD+k+1)*128], qb8[:, k*128:(k+1)*128]),
                                    ch_tr, slot, waits=[v_yqb] if k == 0 else [], tag=f"yqT{app}.{t}.{k}")
                        yqT_done[t] = last
                        G["qb8"] = last
                        G["auxp"] = last
                    evac_done = {}
                    h_done = {}
                    mmdone = {}
                    for n in range(NH):
                        nn = grp * NH + n
                        bw = [bias_tok[nn - 2]] if (nn - 2) in bias_tok else [bq_done]
                        SQ.dma("sync", lambda e, n=n: e.dma_start(out=bgt[0:1, (n % 2)*512:(n % 2)*512+512],
                                in_=bq_d[0:1, n*512:(n+1)*512]), ch_bi, n % 2, waits=bw)
                        ldb = SQ.dma("sync", lambda e, n=n: e.dma_start(out=bvt[0:1, (n % 2)*512:(n % 2)*512+512],
                                in_=bq_d[0:1, H + n*512: H + (n+1)*512]), ch_bi, n % 2)
                        for k in range(KD):
                            wld = [mmdone[(n, k - 2)]] if k >= 2 else (
                                  [mmdone[(n - 1, KD - 2 + k)]] if n >= 1 else
                                  [cc_g, cc_vv] + [yqT_done[t] for t in tiles] + lst(last_group_gv))
                            SQ.dma("sync", lambda e, k=k, n=n: gv_tile_fn(
                                    wstr[:, (k % 2)*512:(k % 2)*512+512], 0, k, n)(e),
                                    ch_g, k % 2, waits=wld, tag=f"gld{app}.{grp}.{n}.{k}")
                            ld = SQ.dma("sync", lambda e, k=k, n=n: gv_tile_fn(
                                    wstr[:, (2 + k % 2)*512:(2 + k % 2)*512+512], 1, k, n)(e),
                                    ch_g, k % 2, tag=f"vld{app}.{grp}.{n}.{k}")
                            for i, t in enumerate(tiles):
                                w0 = [ld, yqT_done[t]]
                                if k == 0:
                                    w0 += [ldb, evac_done.get((t, n - 1))]
                                SQ.emit("tensor", lambda e, i=i, k=k: e.matmul(
                                        PS[i][:, :], yqT[:, (i*KD+k)*128:(i*KD+k+1)*128],
                                        wstr[:, (k % 2)*512:(k % 2)*512+512], start=(k == 0), stop=False),
                                        waits=w0, tag=f"mg{app}.{t}.{n}.{k}")
                                mmdone[(n, k)] = SQ.emit("tensor", lambda e, i=i, k=k: e.matmul(
                                        PS[4+i][:, :], yqT[:, (i*KD+k)*128:(i*KD+k+1)*128],
                                        wstr[:, (2 + k % 2)*512:(2 + k % 2)*512+512], start=(k == 0), stop=False),
                                        inc=s_t, tag=f"mv{app}.{t}.{n}.{k}")
                        prev_h = None
                        for i, t in enumerate(tiles):
                            SQ.emit("tensor", lambda e, i=i, t=t, n=n: e.matmul(
                                    PS[i][:, :], auxG[0:1, t*128:(t+1)*128], bgt[0:1, (n % 2)*512:(n % 2)*512+512],
                                    start=False, stop=True))
                            t_mm = SQ.emit("tensor", lambda e, i=i, t=t, n=n: e.matmul(
                                    PS[4+i][:, :], auxV[0:1, t*128:(t+1)*128], bvt[0:1, (n % 2)*512:(n % 2)*512+512],
                                    start=False, stop=True), inc=s_t)
                            if i == len(tiles) - 1:
                                bias_tok[nn] = t_mm
                            s_ge = SQ.emit("scalar", lambda e, i=i, t=t: e.activation(
                                    gpre[:, :], PS[i][:, :], AF.Silu, scale=ascl[:, t*4:t*4+1]),
                                    waits=[t_mm, prev_h if prev_h is not None else evac_done.get((tiles[-1], n - 1))],
                                    inc=s_s)
                            SQ.emit("vector", lambda e, i=i, t=t: e.tensor_scalar(vpre[:, :], PS[4+i][:, :],
                                    ascl[:, t*4+1:t*4+2], None, op0=ALU.mult), waits=[t_mm])
                            ev = SQ.emit("vector", lambda e, i=i, n=n: e.tensor_mul(
                                    hbuf[:, i*H + n*512: i*H + n*512 + 512], gpre[:, :], vpre[:, :]),
                                    waits=[s_ge], inc=s_v)
                            prev_h = ev
                            evac_done[(t, n)] = ev
                            if n == NH - 1:
                                h_done[t] = ev
                    last_group_gv = mmdone[(NH - 1, KD - 1)]
                    for sl in range(4):
                        G["tr"][sl] = last_group_gv
                    for i, t in enumerate(tiles):
                        hslot = i % HGRP
                        slot = i
                        SQ.emit("vector", lambda e, i=i: e.reduce_max(st8[:, 8:9], hbuf[:, i*H:(i+1)*H], axis=AX,
                                apply_absolute_value=True), waits=[h_done[t]])
                        SQ.emit("vector", lambda e: e.tensor_scalar(st8[:, 8:9], st8[:, 8:9], 1e-5, None, op0=ALU.max))
                        SQ.emit("vector", lambda e: e.reciprocal(st8[:, 9:10], st8[:, 8:9]))
                        SQ.emit("vector", lambda e: e.tensor_scalar(st8[:, 9:10], st8[:, 9:10], 127.0, None, op0=ALU.mult))
                        SQ.emit("vector", lambda e: e.reciprocal(st8[:, 10:11], st8[:, 9:10]))
                        SQ.emit("vector", lambda e: e.tensor_mul(st8[:, 11:12], st8[:, 10:11], gamP[:, 2:3]))
                        SQ.emit("vector", lambda e, t=t: e.tensor_scalar(ascl[:, t*4+2:t*4+3], st8[:, 11:12],
                                0.5, None, op0=ALU.mult))
                        v_hqb = None
                        for cq in range(8):
                            SQ.emit("vector", lambda e, i=i, cq=cq: e.tensor_scalar(q32[:, :],
                                    hbuf[:, i*H + cq*1024: i*H + (cq+1)*1024], st8[:, 9:10], None, op0=ALU.mult))
                            v_hqb = SQ.emit("vector", lambda e, i=i, cq=cq: e.tensor_copy(
                                    hbuf[:, i*H + cq*1024: i*H + (cq+1)*1024], q32[:, :]), inc=s_v)
                        SQ.emit("vector", lambda e: e.tensor_mul(junk[:, 0:1], st8[:, 9:10], gamP[:, 6:7]),
                                waits=[G["junk"]])
                        v_a3 = SQ.emit("vector", lambda e: e.tensor_copy(auxp[:, 0:1], junk[:, 0:1]),
                                waits=[G["auxp"]], inc=s_v)
                        G["junk"] = v_a3
                        tr3 = SQ.dma("scalar", lambda e, t=t: e.dma_start_transpose(
                                auxO[:, t*128:(t+1)*128], auxp[:, :]), ch_tr, slot,
                                waits=[v_a3] + lst(G["tr"][slot]))
                        hqT_last = None
                        for k in range(KH):
                            hw = [v_hqb, tr3] if k == 0 else []
                            if k == 0 and (t - HGRP) * 0 == 0:
                                key = ((t - HGRP) * ND + ND - 1) * KH + KH - 1
                                if key in omm and t >= HGRP:
                                    hw.append(omm[key])
                            hqT_last = SQ.dma("scalar", lambda e, hslot=hslot, k=k, i=i: e.dma_start_transpose(
                                    hqT[:, (hslot*KH+k)*128:(hslot*KH+k+1)*128],
                                    hbuf[:, i*H + k*128: i*H + (k+1)*128]),
                                    ch_tr, slot, waits=hw, tag=f"hqT{app}.{t}.{k}")
                        G["auxp"] = hqT_last
                        xbuf = t % 2
                        xl = SQ.dma("sync", lambda e, t=t, xbuf=xbuf: e.dma_start(
                                out=xw[:, xbuf*D:(xbuf+1)*D], in_=xi_src(t)), ch_x, xbuf,
                                waits=[G["x"][xbuf]], tag=f"xR{app}.{t}")
                        prev_ev = None
                        for n in range(ND):
                            for k in range(KH):
                                oidx = (t * ND + n) * KH + k
                                owait = [omm[oidx - 2]] if (oidx - 2) in omm else [cc_o]
                                ldo = SQ.dma("sync", lambda e, k=k, n=n: e.dma_start(
                                        out=ostr[:, (k % 2)*512:(k % 2)*512+512],
                                        in_=oq_all[k*128:(k+1)*128, n*512:(n+1)*512]),
                                        ch_o, k % 2, waits=owait, tag=f"old{app}.{t}.{n}.{k}")
                                omm[oidx] = SQ.emit("tensor", lambda e, hslot=hslot, k=k, i=i: e.matmul(
                                        PS[i][:, :], hqT[:, (hslot*KH+k)*128:(hslot*KH+k+1)*128],
                                        ostr[:, (k % 2)*512:(k % 2)*512+512], start=(k == 0), stop=False),
                                        waits=[ldo] + ([hqT_last, prev_ev] if k == 0 else []), inc=s_t,
                                        tag=f"mo{app}.{t}.{n}.{k}")
                            t_mo = SQ.emit("tensor", lambda e, t=t, n=n, i=i: e.matmul(
                                    PS[i][:, :], auxO[0:1, t*128:(t+1)*128], bo[0:1, n*512:n*512+512],
                                    start=False, stop=True), inc=s_t)
                            ev = SQ.emit("vector", lambda e, t=t, n=n, xbuf=xbuf, i=i: e.scalar_tensor_tensor(
                                    resb[:, xbuf*D + n*512: xbuf*D + n*512 + 512], PS[i][:, :],
                                    ascl[:, t*4+2:t*4+3], xw[:, xbuf*D + n*512: xbuf*D + n*512 + 512],
                                    op0=ALU.mult, op1=ALU.add),
                                    waits=[t_mo, xl, G["res"][-2] if len(G["res"]) >= 2 and n == 0 else None],
                                    inc=s_v)
                            prev_ev = ev
                        G["x"][xbuf] = prev_ev
                        st = SQ.dma("gpsimd", lambda e, t=t, xbuf=xbuf: e.dma_start(
                                out=res_dst(t), in_=resb[:, xbuf*D:(xbuf+1)*D]), ch_st, t, waits=[prev_ev],
                                tag=f"st{app}.{t}")
                        G["res"].append(st)
                        G["tr"][slot] = omm[max(omm)]
                return G["res"][-NT:]

        res1 = ffn(0, lambda t: x_in[t*128:(t+1)*128, :], lambda t: x1_d[t, :, :])

        wtr_guard = {ws: [SQ.now(s) for s in ch_gp] for ws in range(4)}
        trg_pub = {sl: G["tr"][sl] for sl in range(4)}
        for t in range(NT):
            xbuf = t % 2
            xl = SQ.dma("sync", lambda e, t=t, xbuf=xbuf: e.dma_start(
                    out=xw[:, xbuf*D:(xbuf+1)*D], in_=x1_d[t, :, :]), ch_x, xbuf,
                    waits=[G["x"][xbuf], res1[t]], tag=f"xY{t}")
            s_rstd = ln_stats(xbuf, [xl, G["junk"]])
            G["junk"] = s_rstd
            v_y = SQ.emit("vector", lambda e, xbuf=xbuf: e.tensor_scalar(qb8[:, :], xw[:, xbuf*D:(xbuf+1)*D],
                    st8[:, 2:3], st8[:, 4:5], op0=ALU.subtract, op1=ALU.mult),
                    waits=[s_rstd, G["qb8"]], inc=s_v)
            G["x"][xbuf] = v_y
            last = None
            for k in range(KD):
                ws = k % 4
                wsl = [v_y] if k == 0 else []
                wsl = wsl + lst(wtr_guard[ws])
                if t == 0 and k < 4:
                    wsl += lst(trg_pub[ws])
                lt = SQ.dma("scalar", lambda e, k=k, ws=ws: e.dma_start_transpose(
                        wtr[:, ws*128:(ws+1)*128], qb8[:, k*128:(k+1)*128]),
                        ch_tr, ws, waits=wsl, tag=f"y2T{t}.{k}")
                pub = SQ.dma("gpsimd", lambda e, k=k, t=t, ws=ws: e.dma_start(
                        out=y_in[k*128:(k+1)*128, t*128:(t+1)*128], in_=wtr[:, ws*128:(ws+1)*128]),
                        ch_gp, ws, waits=[lt], tag=f"yp{t}.{k}")
                wtr_guard[ws] = pub
                last = lt
            G["qb8"] = last
        for sl in range(4):
            G["tr"][sl] = wtr_guard[sl]
        cc_y = collective("AllGather", ALU.bypass, [y_in[:, :]], [y_all[:, :]], GROUPS_BATCH,
                          waits=[SQ.now(s) for s in ch_gp])

        with contextlib.ExitStack() as actx:
            A = actx.enter_context
            qkw = A(nc.sbuf_tensor("qkw", [128, NHEAD * S], BF16))
            vt = A(nc.sbuf_tensor("vt", [128, (S // 128) * NHEAD * 128], BF16))
            csb = A(nc.sbuf_tensor("csb", [128, S], BF16))
            ssb = A(nc.sbuf_tensor("ssb", [128, S], BF16))
            mskb = A(nc.sbuf_tensor("mskb", [128, 4 * 512], BF16))
            qsw = A(nc.sbuf_tensor("qsw", [128, S], BF16))
            ystr = A(nc.sbuf_tensor("ystr", [128, 2 * 512], BF16))
            wqk2 = A(nc.sbuf_tensor("wqk2", [128, 2 * 512], BF16))
            expb = A(nc.sbuf_tensor("expb", [128, 2 * 512], BF16))
            dent = A(nc.sbuf_tensor("dent", [1, 1024], F32))
            denb = A(nc.sbuf_tensor("denb", [1, 512], BF16))
            recp = A(nc.sbuf_tensor("recp", [128, 512], F32))
            opn4 = A(nc.sbuf_tensor("opn4", [128, NHEAD * S], BF16))

            at_gate = [G["res"][-1], SQ.now(s_v), SQ.now(s_t), SQ.now(s_s), aw_done, cc_y]
            ld1 = SQ.dma("gpsimd", lambda e: e.dma_start(out=csb[:, :], in_=cs_in[:, :]), ch_b, 0,
                         waits=at_gate)
            SQ.dma("gpsimd", lambda e: e.dma_start(out=ssb[:, :], in_=ss_in[:, :]), ch_b, 0)
            for r in range(4):
                SQ.dma("gpsimd", lambda e, r=r: e.dma_start(out=mskb[:, r*512:(r+1)*512], in_=mk_in[r, :, :]),
                       ch_b, 0)
            at_all = SQ.now(ch_b[0])

            def yread_fn(dst_ap, k, j):
                r0 = (j // 2) * KD * 128 + k * 128
                c0 = (j % 2) * 512
                return lambda e: e.dma_start(out=dst_ap, in_=y_all[r0:r0+128, c0:c0+512])

            v_done = {}
            v_ev = {}
            vw_ld = {}
            for j in range(NQ):
                for k in range(KD):
                    w0 = [at_all] if (j == 0 and k < 2) else \
                         [v_done[(j, k - 2)] if k >= 2 else v_done[(j - 1, KD - 2 + k)]]
                    yl = SQ.dma("sync", lambda e, k=k, j=j: yread_fn(ystr[:, (k % 2)*512:(k % 2)*512+512], k, j)(e),
                            ch_g, k % 2, waits=w0, tag=f"yv{j}.{k}")
                    wl = SQ.dma("sync", lambda e, k=k: e.dma_start(
                            out=wqk2[:, (k % 2)*512:(k % 2)*512+512], in_=wvb_d[k*128:(k+1)*128, :]),
                            ch_o, k % 2, waits=w0, tag=f"wv{j}.{k}")
                    for c4 in range(4):
                        v_done[(j, k)] = SQ.emit("tensor", lambda e, k=k, c4=c4: e.matmul(
                                PS[c4][:, :], ystr[:, (k % 2)*512 + c4*128: (k % 2)*512 + (c4+1)*128],
                                wqk2[:, (k % 2)*512:(k % 2)*512+512], start=(k == 0), stop=(k == KD - 1)),
                                waits=[yl, wl] + ([v_ev[(j - 1, c4)]] if (k == 0 and j > 0) else []),
                                inc=s_t if c4 == 3 else None, tag=f"v{j}.{k}.{c4}")
                if j == 0:
                    v_ev = {}
                for c4 in range(4):
                    cglob = j * 4 + c4
                    v_ev[(j, c4)] = SQ.emit("vector", lambda e, cglob=cglob, c4=c4: e.tensor_copy(
                            vt[:, cglob*NHEAD*128:(cglob+1)*NHEAD*128], PS[c4][:, 0:NHEAD*128]),
                            waits=[v_done[(j, KD - 1)]], inc=s_v, tag=f"ve{cglob}")
            v_all_tok = [v_ev[(NQ - 1, c4)] for c4 in range(4)]

            st_prev = {}
            prev_wave_last = None
            last_rope = None
            for w in range(2):
                heads = (2 * w, 2 * w + 1)
                qk_done = {}
                qk_ev = {}
                for j in range(NQ):
                    for k in range(KD):
                        w0 = [v_done[(NQ - 1, KD - 1)], prev_wave_last] if (j == 0 and k < 2) else \
                             [qk_done[(j, k - 2)] if k >= 2 else qk_done[(j - 1, KD - 2 + k)]]
                        yl = SQ.dma("sync", lambda e, k=k, j=j: yread_fn(
                                ystr[:, (k % 2)*512:(k % 2)*512+512], k, j)(e),
                                ch_g, k % 2, waits=w0, tag=f"yq{w}.{j}.{k}")
                        wl = SQ.dma("sync", lambda e, k=k, w=w: e.dma_start(
                                out=wqk2[:, (k % 2)*512:(k % 2)*512+256],
                                in_=wqkb_d[k*128:(k+1)*128, w*256:(w+1)*256]),
                                ch_o, k % 2, waits=w0, tag=f"wq{w}.{j}.{k}")
                        wl = SQ.dma("sync", lambda e, k=k, w=w: e.dma_start(
                                out=wqk2[:, (k % 2)*512+256:(k % 2)*512+512],
                                in_=wqkb_d[k*128:(k+1)*128, NHEAD*128 + w*256: NHEAD*128 + (w+1)*256]),
                                ch_o, k % 2, tag=f"wk{w}.{j}.{k}")
                        for cc2 in range(4):
                            qk_done[(j, k)] = SQ.emit("tensor", lambda e, k=k, cc2=cc2: e.matmul(
                                    PS[cc2][:, :], wqk2[:, (k % 2)*512 + cc2*128:(k % 2)*512 + (cc2+1)*128],
                                    ystr[:, (k % 2)*512:(k % 2)*512+512], start=(k == 0), stop=(k == KD - 1)),
                                    waits=[yl, wl] + ([qk_ev[(j - 1, cc2)]] if (k == 0 and j > 0) else []),
                                    inc=s_t if cc2 == 3 else None, tag=f"qk{w}.{j}.{k}.{cc2}")
                    for cc2 in range(4):
                        part, hl = divmod(cc2, 2)
                        col0 = (part * 2 + hl) * S
                        qk_ev[(j, cc2)] = SQ.emit("vector", lambda e, col0=col0, j=j, cc2=cc2: e.tensor_copy(
                                qkw[:, col0 + j*512: col0 + j*512 + 512], PS[cc2][:, :]),
                                waits=[qk_done[(j, KD - 1)]], inc=s_v, tag=f"qke{w}.{j}.{cc2}")
                rope_done = {}
                for bidx in range(4):
                    wv0 = [qk_ev[(NQ - 1, (bidx % 2) + 2 * (bidx // 2))], last_rope]
                    SQ.dma("gpsimd", lambda e, bidx=bidx: e.dma_start(out=qsw[0:64, :],
                            in_=qkw[64:128, bidx*S:(bidx+1)*S]), ch_sw, bidx % 2, waits=wv0, tag=f"sw1.{w}.{bidx}")
                    swd = SQ.dma("gpsimd", lambda e, bidx=bidx: e.dma_start(out=qsw[64:128, :],
                            in_=qkw[0:64, bidx*S:(bidx+1)*S]), ch_sw, bidx % 2, tag=f"sw2.{w}.{bidx}")
                    SQ.emit("vector", lambda e: e.tensor_mul(qsw[:, :], qsw[:, :], ssb[:, :]),
                            waits=[swd, at_all])
                    SQ.emit("vector", lambda e, bidx=bidx: e.tensor_mul(qkw[:, bidx*S:(bidx+1)*S],
                            qkw[:, bidx*S:(bidx+1)*S], csb[:, :]), waits=[swd])
                    rope_done[bidx] = SQ.emit("vector", lambda e, bidx=bidx: e.tensor_add(
                            qkw[:, bidx*S:(bidx+1)*S], qkw[:, bidx*S:(bidx+1)*S], qsw[:, :]), inc=s_v,
                            tag=f"ra{w}.{bidx}")
                    last_rope = rope_done[bidx]
                for hl in range(2):
                    h = heads[hl]
                    qcol = hl * S
                    kcol = (2 + hl) * S
                    for j in range(NQ):
                        nk = 4 * (j + 1)
                        for ck in range(nk):
                            pp = ck % 2
                            w0 = [rope_done[hl], rope_done[2 + hl]]
                            if (hl, j, ck - 2, 'e') in st_prev:
                                w0.append(st_prev[(hl, j, ck - 2, 'e')])
                            if ck < 2:
                                if (hl, j - 1) in st_prev:
                                    w0.append(st_prev[(hl, j - 1)])
                                elif (hl - 1, NQ - 1) in st_prev:
                                    w0.append(st_prev[(hl - 1, NQ - 1)])
                                elif prev_wave_last is not None:
                                    w0.append(prev_wave_last)
                            t_sc = SQ.emit("tensor", lambda e, j=j, ck=ck, pp=pp, qcol=qcol, kcol=kcol: e.matmul(
                                    PS[pp][:, :], qkw[:, kcol + ck*128: kcol + (ck+1)*128],
                                    qkw[:, qcol + j*512: qcol + j*512 + 512], start=True, stop=True),
                                    waits=w0, inc=s_t, tag=f"s{w}.{hl}.{j}.{ck}")
                            s_e = SQ.emit("scalar", lambda e, pp=pp: e.activation(
                                    expb[:, pp*512:(pp+1)*512], PS[pp][:, :], AF.Exp, scale=sc128[:, 0:1]),
                                    waits=[t_sc], inc=s_s, tag=f"e{w}.{hl}.{j}.{ck}")
                            if ck >= 4 * j:
                                r = ck - 4 * j
                                s_e = SQ.emit("vector", lambda e, r=r, pp=pp: e.tensor_mul(
                                        expb[:, pp*512:(pp+1)*512], expb[:, pp*512:(pp+1)*512],
                                        mskb[:, r*512:(r+1)*512]), waits=[s_e], inc=s_v)
                            st_prev[(hl, j, ck, 'e')] = s_e
                            SQ.emit("tensor", lambda e, ck=ck, nk=nk, pp=pp: e.matmul(
                                    PS[2][0:1, :], onescb[:, :], expb[:, pp*512:(pp+1)*512],
                                    start=(ck == 0), stop=(ck == nk - 1)), waits=[s_e])
                            SQ.emit("tensor", lambda e, h=h, ck=ck, nk=nk, pp=pp: e.matmul(
                                    PS[3][:, :], vt[:, ck*NHEAD*128 + h*128: ck*NHEAD*128 + (h+1)*128],
                                    expb[:, pp*512:(pp+1)*512], start=(ck == 0), stop=(ck == nk - 1)),
                                    waits=v_all_tok if (w == 0 and hl == 0 and j == 0 and ck == 0) else [],
                                    inc=s_t, tag=f"pv{w}.{hl}.{j}.{ck}")
                        v_d = SQ.emit("vector", lambda e: e.tensor_copy(dent[:, 0:512], PS[2][0:1, :]),
                                waits=[SQ.now(s_t)], inc=s_v)
                        SQ.emit("vector", lambda e: e.reciprocal(dent[:, 512:1024], dent[:, 0:512]))
                        v_r = SQ.emit("vector", lambda e: e.tensor_copy(denb[:, :], dent[:, 512:1024]), inc=s_v)
                        t_rep = SQ.emit("tensor", lambda e: e.matmul(PS[4][:, :], ones1b[:, :], denb[:, :],
                                start=True, stop=True), waits=[v_r], inc=s_t)
                        SQ.emit("vector", lambda e: e.tensor_copy(recp[:, :], PS[4][:, :]), waits=[t_rep])
                        st_prev[(hl, j)] = SQ.emit("vector", lambda e, j=j, h=h: e.tensor_mul(
                                opn4[:, h*S + j*512: h*S + j*512 + 512], PS[3][:, :], recp[:, :]), inc=s_v)
                prev_wave_last = st_prev[(1, NQ - 1)]
                wave_opn_done = [st_prev[(hl2, NQ - 1)] for hl2 in range(2)]
                st_prev = {}
            ppmm = {}
            pp_st = {}
            bank_ev = {}
            for mb in range(S // 128 // 8):
                for n in range(ND):
                    for kh in range(NHEAD):
                        wid = (mb * ND + n) * NHEAD + kh
                        wowait = [ppmm[wid - 2]] if wid >= 2 else [aw_done, prev_wave_last]
                        wl = SQ.dma("sync", lambda e, kh=kh, n=n, wid=wid: e.dma_start(
                                out=wqk2[:, (wid % 2)*512:(wid % 2)*512+512],
                                in_=wob_d[kh*128:(kh+1)*128, n*512:(n+1)*512]),
                                ch_o, wid % 2, waits=wowait, tag=f"wopp{mb}.{n}.{kh}")
                        for mi in range(8):
                            m = mb * 8 + mi
                            w0 = [wl]
                            if kh == 0:
                                w0.append(bank_ev.get(mi))
                                if mb == 0 and n == 0:
                                    w0.append(prev_wave_last)
                            ppmm[wid] = SQ.emit("tensor", lambda e, m=m, kh=kh, mi=mi, wid=wid: e.matmul(
                                    PS[mi][:, :], opn4[:, kh*S + m*128: kh*S + (m+1)*128],
                                    wqk2[:, (wid % 2)*512:(wid % 2)*512+512],
                                    start=(kh == 0), stop=(kh == NHEAD - 1)),
                                    waits=w0, inc=s_t if mi == 7 else None, tag=f"pp{m}.{n}.{kh}")
                    t_pp = ppmm[(mb * ND + n) * NHEAD + NHEAD - 1]
                    for mi in range(8):
                        m = mb * 8 + mi
                        gi = (mb * ND + n) * 8 + mi
                        ev = SQ.emit("vector", lambda e, mi=mi: e.tensor_copy(
                                qb8[:, (mi % 4)*512:(mi % 4)*512+512], PS[mi][:, :]),
                                waits=[t_pp, G["qb8"] if gi == 0 else None, pp_st.get(gi - 4)],
                                inc=s_v, tag=f"ppev{m}.{n}")
                        bank_ev[mi] = ev
                        pp_st[gi] = SQ.dma("gpsimd", lambda e, m=m, n=n, mi=mi: e.dma_start(
                                out=pp_d[m*128:(m+1)*128, n*512:(n+1)*512],
                                in_=qb8[:, (mi % 4)*512:(mi % 4)*512+512]),
                                ch_gp, gi % 4, waits=[ev], tag=f"ppst{m}.{n}")
            G["qb8"] = SQ.now(s_v)
            cc_op = collective("ReduceScatter", ALU.add, [pp_d[:, :]], [rs_d[:, :]], GROUPS_BATCH,
                               waits=[SQ.now(s) for s in ch_gp])

        res2 = []
        for t in range(NT):
            xbuf = t % 2
            xl = SQ.dma("sync", lambda e, t=t, xbuf=xbuf: e.dma_start(
                    out=xw[:, xbuf*D:(xbuf+1)*D], in_=x1_d[t, :, :]), ch_x, xbuf,
                    waits=[G["x"][xbuf]], tag=f"xO{t}")
            rl = SQ.dma("sync", lambda e, t=t: e.dma_start(out=qb8[:, :], in_=rs_d[t*128:(t+1)*128, :]),
                    ch_o, t % 2, waits=[cc_op, G["qb8"]], tag=f"rsl{t}")
            ev = SQ.emit("vector", lambda e, xbuf=xbuf: e.tensor_add(
                    resb[:, xbuf*D:(xbuf+1)*D], xw[:, xbuf*D:(xbuf+1)*D], qb8[:, :]),
                    waits=[xl, rl, G["res"][-2] if len(G["res"]) >= 2 else None], inc=s_v, tag=f"res2.{t}")
            G["qb8"] = ev
            G["x"][xbuf] = ev
            st = SQ.dma("gpsimd", lambda e, t=t, xbuf=xbuf: e.dma_start(
                    out=x2_d[t, :, :], in_=resb[:, xbuf*D:(xbuf+1)*D]), ch_st, t, waits=[ev], tag=f"st2.{t}")
            G["res"].append(st)
            res2.append(st)

        res3 = ffn(1, lambda t: x2_d[t, :, :], lambda t: out_d[t*128:(t+1)*128, :],
                   extra_first=[res2[-1]])

        SQ.emit("gpsimd", lambda e: e.nop(), waits=[SQ.now(s) for s in ch_st])
        SQ.build(block)
    return nc


def _host_consts():
    i = np.arange(64, dtype=np.float32)
    inv = (1.0 / (np.float32(10000.0) ** (np.float32(2.0) * i / np.float32(128.0)))).astype(np.float32)
    pos = np.arange(S, dtype=np.float32)
    th = np.outer(pos, inv).astype(np.float32)
    cos = np.cos(th).T.astype(np.float32)
    sin = np.sin(th).T.astype(np.float32)
    cs = np.concatenate([cos, cos], 0)
    ss2 = np.concatenate([-sin, sin], 0)
    mask = np.zeros((4, 128, 512), np.float32)
    for r in range(4):
        kidx = np.arange(128)[:, None] + r * 128
        qidx = np.arange(512)[None, :]
        mask[r] = (kidx <= qidx).astype(np.float32)
    return cs, ss2, mask.astype(ml_dtypes.bfloat16)


def kernel(L0_attn_x, L0_attn_Wqkv, L0_attn_Wo, L0_ffn_gate_w, L0_ffn_gate_b,
           L0_ffn_val_w, L0_ffn_val_b, L0_ffn_out_w, L0_ffn_out_b):
    x = np.ascontiguousarray(np.asarray(L0_attn_x, np.float32).reshape(B * S, D))
    Wqkv = np.asarray(L0_attn_Wqkv, np.float32)
    Wo = np.ascontiguousarray(np.asarray(L0_attn_Wo, np.float32))
    gw = np.asarray(L0_ffn_gate_w, np.float32)
    vw = np.asarray(L0_ffn_val_w, np.float32)
    ow = np.asarray(L0_ffn_out_w, np.float32)
    gb = np.asarray(L0_ffn_gate_b, np.float32).reshape(1, H)
    vb = np.asarray(L0_ffn_val_b, np.float32).reshape(1, H)
    ob = np.asarray(L0_ffn_out_b, np.float32).reshape(1, D)
    cs, ss2, mask = _host_consts()

    perm = np.concatenate([np.arange(0, 128, 2), np.arange(1, 128, 2)])
    in_maps = []
    for c in range(NCORES):
        heads = [4 * (c % 4) + hh for hh in range(NHEAD)]
        wq = np.concatenate([Wqkv[:, 128*h:128*(h+1)][:, perm] for h in heads], axis=1)
        wk = np.concatenate([Wqkv[:, D + 128*h:D + 128*(h+1)][:, perm] for h in heads], axis=1)
        wv = np.concatenate([Wqkv[:, 2*D + 128*h:2*D + 128*(h+1)] for h in heads], axis=1)
        in_maps.append(dict(
            x=np.ascontiguousarray(x[c*TPC:(c+1)*TPC]),
            gw=np.ascontiguousarray(gw[c*SLICE:(c+1)*SLICE]),
            vw=np.ascontiguousarray(vw[c*SLICE:(c+1)*SLICE]),
            ow=np.ascontiguousarray(ow[:, c*SLICE:(c+1)*SLICE]),
            gb=gb, vb=vb, ob=ob,
            wq=np.ascontiguousarray(wq), wk=np.ascontiguousarray(wk), wv=np.ascontiguousarray(wv),
            wo=np.ascontiguousarray(Wo[512*(c % 4): 512*(c % 4) + 512, :]), cs=cs, ss2=ss2, mask=mask,
        ))
    nc = build_nc()
    res = run_bass_kernel_spmd(nc, in_maps, core_ids=list(range(NCORES)))
    out = np.concatenate([np.asarray(res.results[c]["out"], np.float32) for c in range(NCORES)], axis=0)
    return out.reshape(B, S, D)
